# version 02
# baseline (speedup 1.0000x reference)
import math
from contextlib import ExitStack

import numpy as np
import ml_dtypes
import bass_rust
import concourse.bass as bass
import concourse.mybir as mybir
from concourse.bass_utils import run_bass_kernel_spmd

F32 = mybir.dt.float32
BF16 = mybir.dt.bfloat16
I32 = mybir.dt.int32
AF = mybir.ActivationFunctionType
ALU = mybir.AluOpType
AX = mybir.AxisListType

LN2 = float(np.log(2.0))
B, S, D = 4, 4096, 1024
NH, HD = 16, 64
T_OWN = 2048
NT = 16
NSL = 4
NCH = 8
DH = 4096
NHC = 32
KCH = 32
STRIPW = 6016
EBW = 6272
N_K = D * T_OWN
N_V = T_OWN * 1040
N_HALO = 2 * D
SLOT = N_K + N_V + N_HALO
RG_PAIRS = [[0, 1], [2, 3], [4, 5], [6, 7]]


def mkap(ap, offset, pattern):
    ap2 = ap.copy()
    ap2.ap = bass_rust.VecI64Pair([tuple(p) for p in pattern])
    ap2.offset = offset
    return ap2


def _bucket(rel):
    nb = 16
    out = np.where(rel > 0, nb, 0).astype(np.int64)
    a = np.abs(rel)
    me = nb // 2
    small = a < me
    tmp = np.log(np.maximum(a, 1).astype(np.float64) / me)
    large = me + (tmp / math.log(128.0 / me) * (nb - me)).astype(np.int64)
    large = np.minimum(large, nb - 1)
    return out + np.where(small, a, large)


class Bld:
    ENGS = ("sync", "tensor", "vector", "scalar", "gpsimd")

    def __init__(self, nc, blk):
        self.nc = nc
        self.blk = blk
        self.streams = {e: [] for e in self.ENGS}
        self.sem_handles = {}
        self.counts = {}

    CANON = None

    def canon(self, name):
        import re as _re
        m = _re.match(r"^([A-Z])([0-3])(.*)$", name)
        if m:
            name = m.group(1) + m.group(3)
        m = _re.match(r"^(Qlb|Ppl|Pfl|Fwl|Fzl|Eel)([0-3])$", name)
        if m:
            return "lw" + m.group(2)
        m = _re.match(r"^(Qmmu|Ppm|Pfm|Fwm|Fzm|Eem)([0-3])$", name)
        if m:
            return "lm" + m.group(2)
        if name in ("Lxtr", "Pttr", "Pftr", "Px1tr", "Fx2tr", "Vvtr", "Eqtr", "Famt"):
            return "xtr"
        if name in ("Lxnev", "Ptev", "Pfev", "Px1ev", "Fx2ev", "Vvev", "Eqc", "Famv"):
            return "xev"
        if name.endswith("_rc"):
            return "rs_rc"
        if name.endswith("_sq") and name != "Lsq":
            return "rs_sq"
        if name.startswith("C"):
            return "Cn" if name in ("Cnd", "Cnm", "Cns") else "Cc"
        if name.startswith("E"):
            return "Ee"
        if name in ("Arb", "Arcp", "Arw", "Actxn", "Acst"):
            return "Actx"
        if name in ("Asl0", "Asl1"):
            return "Asl"
        if name in ("Pcld", "Pqz", "Pxc", "Pxcons", "Pxnb", "Px1c", "Px1fst", "Px4", "Ppmm"):
            return "Pm"
        if name in ("Fdq", "Fhm0", "Fhev", "Fhed", "Fcv", "Fsi", "Fhmx", "Fqs", "Fd2",
                    "Fqsw", "Fqsr", "Fqd", "Fx2c"):
            return "Ff"
        if name in ("Kst", "Vst"):
            return "KVst"
        for pre, grp in (("wp_", "su_w"), ("wd_", "su_w"), ("ma_", "su_m"), ("wa_", "su_m"),
                         ("oh_", "su_e"), ("ebr_", "su_e"), ("sxw", "su_s"), ("sw1", "su_s"),
                         ("sw2", "su_s"), ("mar", "su_s"), ("maw", "su_s"), ("mav", "su_s"),
                         ("mas", "su_s"), ("am", "am"), ("hb", "hb"), ("hl", "hl"),
                         ("q_", "qp"), ("vst", "vst"), ("scg", "am"), ("tapd", "tap")):
            if name.startswith(pre):
                return grp
        return name

    def sem(self, name):
        name = self.canon(name)
        if name not in self.sem_handles:
            self.sem_handles[name] = self.nc.alloc_semaphore(name=name)
            self.counts[name] = 0
        return self.sem_handles[name]

    def emit(self, eng, fn, inc=None, k=1):
        if inc is None:
            self.streams[eng].append(fn)
            return
        h = self.sem(inc)
        self.counts[self.canon(inc)] += k

        def wrapped(e, fn=fn, h=h, k=k):
            fn(e).then_inc(h, k)
        self.streams[eng].append(wrapped)

    def wait(self, eng, semname, val=None):
        h = self.sem(semname)
        v = self.counts[self.canon(semname)] if val is None else val
        if v <= 0:
            return
        self.streams[eng].append(lambda e, h=h, v=v: e.wait_ge(h, v))

    def cnt(self, semname):
        self.sem(semname)
        return self.counts[self.canon(semname)]

    def flush(self):
        for eng in self.ENGS:
            stream = self.streams[eng]
            if not stream:
                continue
            deco = getattr(self.blk, eng)

            def body(e, stream=tuple(stream)):
                for fn in stream:
                    fn(e)
            deco(body)
            self.streams[eng] = []


def build_nc(upto="ALL", taps=(), skip_prep=False):
    nc = bass.Bass("TRN2", target_bir_lowering=False, debug=False)
    dtt = {}

    def inp(name, shape, dtype=F32):
        dtt[name] = nc.dram_tensor(name, list(shape), dtype, kind="ExternalInput").ap()

    def outp(name, shape, dtype=F32):
        dtt[name] = nc.dram_tensor(name, list(shape), dtype, kind="ExternalOutput").ap()

    def intern(name, shape, dtype=F32):
        dtt[name] = nc.dram_tensor(name, list(shape), dtype, kind="Internal").ap()

    inp("x_own", (T_OWN, D))
    inp("Wqkv0", (3 * D, D))
    inp("qg", (HD,)); inp("qb", (HD,)); inp("kg", (HD,)); inp("kb", (HD,))
    inp("Wproj0", (D, D)); inp("bproj0", (D,))
    inp("Wffn0", (D, D)); inp("bffn0", (D,))
    inp("Wqkv1", (D, 3 * D)); inp("bqkv1", (3 * D,))
    inp("Wdense1", (D, D)); inp("bdense1", (D,))
    inp("tabT", (32, NH))
    inp("w1", (DH, D)); inp("convw", (DH, 3)); inp("convb", (DH,))
    inp("w2", (D, DH))
    inp("ohT", (32, EBW))
    inp("ident", (128, 128), BF16)
    inp("identf", (128, 128))
    inp("colind2_in", (2, 128))
    inp("hmask", (2,))
    outp("y", (T_OWN, D))

    intern("wq0T_h", (24 * NCH * 128, 128), BF16)
    intern("wp0T_h", (NCH * NCH * 128, 128), BF16)
    intern("wf0q_h", (NCH * NCH * 128, 128), BF16)
    intern("wq1_h", (NCH * 24 * 128, 128), BF16)
    intern("wd1_h", (NCH * NCH * 128, 128), BF16)
    intern("w1q_h", (NHC * NCH * 128, 128), BF16)
    intern("w2q_h", (NCH * NHC * 128, 128), BF16)
    intern("ebr_h", (NH * EBW,), BF16)
    intern("ag_in", (SLOT,), BF16)
    intern("ag_out", (2 * SLOT,), BF16)
    intern("ar_in", (1, 1)); intern("ar_out", (1, 1))
    intern("x1_h", (T_OWN, D))
    intern("x1f_h", (D, T_OWN))
    intern("ctx_h", (NCH * 128, T_OWN), BF16)
    intern("scr_h", (4096,))

    with ExitStack() as OUTER:
        blk = OUTER.enter_context(nc.Block())
        b = Bld(nc, blk)

        def sb(st, name, shape, dtype=F32):
            return st.enter_context(nc.sbuf_tensor(name, list(shape), dtype))

        P = OUTER
        qg_s = sb(P, "qg_s", (128, 1)); qb_s = sb(P, "qb_s", (128, 1))
        kg_s = sb(P, "kg_s", (128, 1)); kb_s = sb(P, "kb_s", (128, 1))
        bproj_s = sb(P, "bproj_s", (128, NCH))
        bffn_s = sb(P, "bffn_s", (128, NCH))
        bqkv1_s = sb(P, "bqkv1_s", (128, 24))
        bdense_s = sb(P, "bdense_s", (128, NCH))
        convb_s = sb(P, "convb_s", (128, NHC))
        convw_s = sb(P, "convw_s", (128, 3 * NHC))
        tab_s = sb(P, "tab_s", (32, NH))
        etab_s = sb(P, "etab_s", (32, NH))
        id_s = sb(P, "id_s", (128, 128), BF16)
        idf_s = sb(P, "idf_s", (128, 128))
        onesf2 = sb(P, "onesf2", (128, 2))
        colind2 = sb(P, "colind2", (2, 128))
        negC = sb(P, "negC", (128, 8))
        hm_s = sb(P, "hm_s", (1, 2))
        amax_acc = sb(P, "amax_acc", (128, 1))
        aw_acc = sb(P, "aw_acc", (128, 1))
        sxw_col = sb(P, "sxw_col", (128, 1))
        dqw_col = sb(P, "dqw_col", (128, 1))
        sxq_col = sb(P, "sxq_col", (128, 1))
        scg_col = sb(P, "scg_col", (128, 1))
        sw1_col = sb(P, "sw1_col", (128, 1)); dq1_col = sb(P, "dq1_col", (128, 1))
        sw2_col = sb(P, "sw2_col", (128, 1)); dq2_col = sb(P, "dq2_col", (128, 1))
        cc1_col = sb(P, "cc1_col", (128, 1))
        cc2_col = sb(P, "cc2_col", (128, 1))
        c_m6 = sb(P, "c_m6", (128, 1)); c_3l2 = sb(P, "c_3l2", (128, 1))
        c_m3l2 = sb(P, "c_m3l2", (128, 1))
        sm1 = sb(P, "sm1", (128, 1)); sm2 = sb(P, "sm2", (128, 1))
        sm3 = sb(P, "sm3", (128, 1)); sm4 = sb(P, "sm4", (128, 1))
        qm2 = sb(P, "qm2", (2, NCH)); km2 = sb(P, "km2", (2, NCH))
        row512 = sb(P, "row512", (1, 512))
        rowB0 = sb(P, "rowB0", (1, 1024), BF16)
        rowB1 = sb(P, "rowB1", (1, 1024), BF16)
        haloL = sb(P, "haloL", (1, D))
        haloR = sb(P, "haloR", (1, D))
        t0 = sb(P, "t0", (128, 1024))
        t1 = sb(P, "t1", (128, 1024))
        sq_s = sb(P, "sq_s", (128, 1024))
        ti_hold = [None]
        tb = sb(P, "tb", (128, 1024), BF16)
        xs = sb(P, "xs", (128, D))
        xn_tm = sb(P, "xn_tm", (128, D))
        lhsT = [sb(P, f"lhsT{i}", (128, 128), BF16) for i in range(4)]
        lhsT_tags = []

        def lhsT_guard(newtags):
            for tname in lhsT_tags:
                b.wait("sync", tname)
            lhsT_tags.clear()
            lhsT_tags.extend(newtags)
        sct0 = sb(P, "sct0", (128, 512))
        sct1 = sb(P, "sct1", (128, 512))
        hbuf = sb(P, "hbuf", (128, 514))
        qrow = sb(P, "qrow", (1, 1024))
        qcol = sb(P, "qcol", (128, 8), BF16)
        hedge5 = sb(P, "hedge5", (128, 5 * NHC))
        onesb1 = sb(P, "onesb1", (1, 1), BF16)
        onesf1r = sb(P, "onesf1r", (1, 128))
        hedgeL = sb(P, "hedgeL", (128, NHC))
        amax4 = sb(P, "amax4", (128, 4))
        mu0 = sb(P, "mu0", (128, NT)); r0 = sb(P, "r0", (128, NT))
        mu1 = sb(P, "mu1", (128, NT)); r1 = sb(P, "r1", (128, NT))
        smA = sb(P, "smA", (128, 1)); smB = sb(P, "smB", (128, 1))
        smC = sb(P, "smC", (128, 1)); smD = sb(P, "smD", (128, 1))
        deq4 = sb(P, "deq4", (128, 4))

        scA = P.enter_context(nc.psum_tensor("scA", [128, 1024], F32))
        scB = P.enter_context(nc.psum_tensor("scB", [128, 1024], F32))
        pc0 = P.enter_context(nc.psum_tensor("pc0", [128, 512], F32))
        pc1 = P.enter_context(nc.psum_tensor("pc1", [128, 512], F32))
        pg0 = P.enter_context(nc.psum_tensor("pg0", [128, 512], F32))
        ptr = P.enter_context(nc.psum_tensor("ptr", [128, 1024], BF16))

        def dma(eng, out, in_, inc=None, slow=False):
            if slow:
                b.emit(eng, lambda e, o=out, i=in_: e.dma_start(out=o, in_=i, allow_slow_non_contiguous=True),
                       inc=inc, k=16)
            else:
                b.emit(eng, lambda e, o=out, i=in_: e.dma_start(out=o, in_=i), inc=inc, k=16)

        def ts(eng, out, in0, s1, s2, op0, op1=None, inc=None, drain=False):
            if op1 is None:
                def f(e, o=out, i=in0, a=s1, dr=drain):
                    r_ = e.tensor_scalar(o, i, a, None, op0)
                    return e.drain() if dr else r_
            else:
                def f(e, o=out, i=in0, a=s1, c=s2, dr=drain):
                    r_ = e.tensor_scalar(o, i, a, c, op0, op1)
                    return e.drain() if dr else r_
            b.emit(eng, f, inc=inc)

        def tt(eng, out, in0, in1, op, inc=None, drain=False):
            def f(e, o=out, a=in0, c=in1, dr=drain):
                r_ = e.tensor_tensor(o, a, c, op)
                return e.drain() if dr else r_
            b.emit(eng, f, inc=inc)

        def stt(eng, out, in0, scalar, in1, op0, op1, inc=None):
            b.emit(eng, lambda e, o=out, a=in0, s=scalar, c=in1:
                   e.scalar_tensor_tensor(o, a, s, c, op0, op1), inc=inc)

        def act(out, in_, func, bias=None, scale=1.0, accum=None, inc=None):
            def f(e, o=out, i=in_, fn=func, bi=bias, sc=scale, ac=accum):
                kw = {}
                if bi is not None:
                    kw["bias"] = bi
                if ac is not None:
                    kw["accum_out"] = ac
                return e.activation(o, i, fn, scale=sc, **kw)
            b.emit("scalar", f, inc=inc)

        def copy(eng, out, in_, inc=None):
            if eng == "scalar":
                b.emit(eng, lambda e, o=out, i=in_: e.copy(o, i), inc=inc)
            else:
                b.emit(eng, lambda e, o=out, i=in_: e.tensor_copy(o, i), inc=inc)

        def mm(out, lh, rh, start, stop, tp=None, inc=None):
            b.emit("tensor", lambda e, o=out, l=lh, r=rh, st=start, sp=stop, t=tp:
                   e.matmul(o, l, r, start=st, stop=sp, tile_position=t), inc=inc)

        def transp(out, in_, ident, inc=None):
            b.emit("tensor", lambda e, o=out, i=in_, ii=ident: e.transpose(o, i, ii), inc=inc)

        def red(out, in_, op, absval=False, inc=None):
            def f(e, o=out, i=in_, p=op, a=absval):
                e.tensor_reduce(o, i, AX.X, p, apply_absolute_value=a)
                return e.drain()
            b.emit("vector", f, inc=inc)

        def recip(out, in_, inc=None):
            def f(e, o=out, i=in_):
                e.reciprocal(o, i)
                return e.drain()
            b.emit("vector", f, inc=inc)

        def rsqrt_col(dst, vcol, tag, sc1=None, sc2=None):
            P_ = vcol.shape[0]
            w = vcol.shape[1]
            if sc1 is None:
                sc1 = t1[0:P_, 768:768 + w]
            if sc2 is None:
                sc2 = t1[0:P_, 832:832 + w]
            recip(sc1, vcol, inc=tag + "_rc")
            b.wait("scalar", tag + "_rc")
            act(dst, sc1, AF.Sqrt, inc=tag + "_sq")
            b.wait("vector", tag + "_sq")
            tt("vector", sc2, dst, dst, ALU.mult, drain=True)
            tt("vector", sc2, vcol, sc2, ALU.mult, drain=True)
            ts("vector", sc2, sc2, -0.5, 1.5, ALU.mult, ALU.add, drain=True)
            tt("vector", dst, dst, sc2, ALU.mult, drain=True)

        def memset(eng, ap, val, inc=None):
            b.emit(eng, lambda e, a=ap, v=val: e.memset(a, v), inc=inc)

        def rne(out, in_, w):
            ti = ti_hold[0]
            copy("vector", ti[:, 0:w], in_)
            copy("vector", out, ti[:, 0:w])

        def rr(apx, bdim):
            return apx.rearrange("(a b) -> a b", b=bdim)

        def tap(name, src_ap, shape, dtype=F32, prewait=()):
            if name not in taps:
                return
            outp("dbg_" + name, shape, dtype)
            for w in prewait:
                b.wait("sync", w)
            o = dtt["dbg_" + name]
            dma("sync", o[:, :] if len(shape) == 2 else o[:], src_ap, inc="tapd")
            b.wait("sync", "tapd")

        def pmax_col(dst11, col, tag):
            b.wait("tensor", tag + "_c", b.cnt(tag + "_c"))
            transp(pg0[0:1, 0:128], col, idf_s[:, :], inc=tag + "_t")
            b.wait("vector", tag + "_t")
            red(dst11, pg0[0:1, 0:128], ALU.max, inc=tag + "_c")

        def psum_col(dst11, col, tag):
            b.wait("tensor", tag + "_c", b.cnt(tag + "_c"))
            transp(pg0[0:1, 0:128], col, idf_s[:, :], inc=tag + "_t")
            b.wait("vector", tag + "_t")
            red(dst11, pg0[0:1, 0:128], ALU.add, inc=tag + "_c")

        def bcast11(src1w, dst, tag, w=1):
            b.wait("tensor", "bcg", b.cnt("bcg"))
            mm(pg0[:, 256:256 + w], onesf1r[0:1, :], src1w, True, True, inc=tag + "bq")
            b.wait("vector", tag + "bq")
            b.emit("vector", lambda e, d=dst, sp=pg0[:, 256:256 + w]:
                   (e.tensor_copy(d, sp), e.drain())[-1], inc="bcg")
            b.wait("vector", "bcg")

        dma("sync", qg_s[0:64, :], rr(dtt["qg"], 1), inc="ld")
        dma("sync", qg_s[64:128, :], rr(dtt["qg"], 1), inc="ld")
        dma("sync", qb_s[0:64, :], rr(dtt["qb"], 1), inc="ld")
        dma("sync", qb_s[64:128, :], rr(dtt["qb"], 1), inc="ld")
        dma("sync", kg_s[0:64, :], rr(dtt["kg"], 1), inc="ld")
        dma("sync", kg_s[64:128, :], rr(dtt["kg"], 1), inc="ld")
        dma("sync", kb_s[0:64, :], rr(dtt["kb"], 1), inc="ld")
        dma("sync", kb_s[64:128, :], rr(dtt["kb"], 1), inc="ld")
        dma("sync", bproj_s[:, :], mkap(dtt["bproj0"], 0, [(1, 128), (128, NCH)]), inc="ld", slow=True)
        dma("sync", bffn_s[:, :], mkap(dtt["bffn0"], 0, [(1, 128), (128, NCH)]), inc="ld", slow=True)
        dma("sync", bqkv1_s[:, :], mkap(dtt["bqkv1"], 0, [(1, 128), (128, 24)]), inc="ld", slow=True)
        dma("sync", bdense_s[:, :], mkap(dtt["bdense1"], 0, [(1, 128), (128, NCH)]), inc="ld", slow=True)
        dma("sync", convb_s[:, :], mkap(dtt["convb"], 0, [(1, 128), (128, NHC)]), inc="ld", slow=True)
        dma("sync", convw_s[:, :], mkap(dtt["convw"], 0, [(3, 128), (384, NHC), (1, 3)]), inc="ld", slow=True)
        dma("sync", tab_s[:, :], dtt["tabT"][:, :], inc="ld")
        dma("sync", id_s[:, :], dtt["ident"][:, :], inc="ld")
        dma("sync", idf_s[:, :], dtt["identf"][:, :], inc="ld")
        dma("sync", hm_s[:, :], rr(dtt["hmask"], 2), inc="ld")
        dma("sync", colind2[:, :], dtt["colind2_in"][:, :], inc="ld")

        memset("vector", onesf2[:, :], 0.0)
        memset("vector", onesf2[0:64, 0:1], 1.0)
        memset("vector", onesf2[64:128, 1:2], 1.0)
        memset("vector", c_m6[:, :], -6.0)
        memset("vector", c_3l2[:, :], 3.0 * LN2)
        memset("vector", c_m3l2[:, :], -3.0 * LN2)
        memset("vector", amax_acc[:, :], 0.0)
        memset("vector", onesb1[0:1, 0:1], 1.0)
        memset("vector", onesf1r[0:1, :], 1.0)
        memset("vector", negC[:, :], 0.0, inc="consts")

        b.wait("scalar", "ld")
        b.wait("scalar", "consts")
        act(etab_s[:, :], tab_s[:, :], AF.Exp, inc="etab")

        b.wait("tensor", "etab")
        for mc in range(0 if skip_prep else EBW // 128):
            b.wait("sync", "ebr_mm", b.cnt("ebr_mm"))
            dma("sync", t0[0:32, 0:128], dtt["ohT"][:, mc * 128:(mc + 1) * 128], inc="oh_ld")
            b.wait("tensor", "oh_ld")
            b.wait("tensor", "ebr_c1", b.cnt("ebr_c1"))
            mm(pg0[:, 0:16], t0[0:32, 0:128], etab_s[:, :], True, True, inc="ebr_mm")
            b.wait("scalar", "ebr_mm")
            copy("scalar", tb[:, 0:16], pg0[:, 0:16], inc="ebr_c1")
            b.wait("tensor", "ebr_c1")
            b.wait("tensor", "ebr_c2", b.cnt("ebr_c2"))
            transp(ptr[0:16, 0:128], tb[:, 0:16], id_s[:, :], inc="ebr_tr")
            b.wait("scalar", "ebr_tr")
            copy("scalar", tb[0:16, 128:256], ptr[0:16, 0:128], inc="ebr_c2")
            b.wait("sync", "ebr_c2")
            dma("sync", rr(dtt["ebr_h"], EBW)[:, mc * 128:(mc + 1) * 128], tb[0:16, 128:256], inc="ebr_st")
        b.wait("sync", "ebr_st")

        def fp8q(out_bf, in_f32, w, scol):
            ts("vector", t0[:, 0:w], in_f32, scol[:, :], None, ALU.mult, inc="q_y")
            b.wait("scalar", "q_y")
            act(t1[:, 0:w], t0[:, 0:w], AF.Abs, inc="q_ab")
            b.wait("vector", "q_ab")
            ts("vector", t1[:, 0:w], t1[:, 0:w], 1e-30, None, ALU.max, inc="q_pre")
            b.wait("scalar", "q_pre")
            act(t1[:, 0:w], t1[:, 0:w], AF.Ln, inc="q_ln")
            b.wait("vector", "q_ln")
            ts("vector", t1[:, 0:w], t1[:, 0:w], 1.0 / LN2, -0.5, ALU.mult, ALU.add)
            rne(t1[:, 0:w], t1[:, 0:w], w)
            ts("vector", t1[:, 0:w], t1[:, 0:w], c_m6[:, :], None, ALU.max, inc="q_e")
            b.wait("scalar", "q_e")
            act(sq_s[:, 0:w], t1[:, 0:w], AF.Exp, bias=c_3l2[:, :], scale=-LN2, inc="q_e1")
            act(t1[:, 0:w], t1[:, 0:w], AF.Exp, bias=c_m3l2[:, :], scale=LN2, inc="q_e2")
            b.wait("vector", "q_e1")
            tt("vector", t0[:, 0:w], t0[:, 0:w], sq_s[:, 0:w], ALU.mult)
            rne(t0[:, 0:w], t0[:, 0:w], w)
            b.wait("vector", "q_e2")
            tt("vector", out_bf, t0[:, 0:w], t1[:, 0:w], ALU.mult)

        def i8q_col(out_bf, in_f32, w, scol):
            ts("vector", t0[:, 0:w], in_f32, scol, None, ALU.mult)
            rne(t0[:, 0:w], t0[:, 0:w], w)
            ts("vector", out_bf, t0[:, 0:w], -128.0, 127.0, ALU.max, ALU.min)

        def i8q_tile(out_bf, in_f32, w, stile):
            tt("vector", t0[:, 0:w], in_f32, stile, ALU.mult)
            rne(t0[:, 0:w], t0[:, 0:w], w)
            ts("vector", out_bf, t0[:, 0:w], -128.0, 127.0, ALU.max, ALU.min)

        def prep_T(src, dst, nout, nin, quant=None, swcol=None):
            for och in range(nout):
                for ich in range(nin):
                    b.wait("sync", "wp_st", b.cnt("wp_st"))
                    dma("sync", wtmp[:, 0:128], src[och * 128:(och + 1) * 128, ich * 128:(ich + 1) * 128], inc="wp_ld")
                    b.wait("vector", "wp_ld")
                    if quant == "fp8":
                        fp8q(wtb[:, 0:128], wtmp[:, 0:128], 128, swcol)
                        b.emit("vector", lambda e: e.tensor_copy(sm1[0:1, :], sm1[0:1, :]), inc="wp_c")
                    elif quant == "tern":
                        ts("vector", t0[:, 0:128], wtmp[:, 0:128], swcol[:, :], None, ALU.mult)
                        rne(t0[:, 0:128], t0[:, 0:128], 128)
                        ts("vector", wtb[:, 0:128], t0[:, 0:128], -1.0, 1.0, ALU.max, ALU.min, inc="wp_c")
                    else:
                        copy("vector", wtb[:, 0:128], wtmp[:, 0:128], inc="wp_c")
                    b.wait("tensor", "wp_c")
                    b.wait("tensor", "wp_ev", b.cnt("wp_ev"))
                    transp(ptr[:, 0:128], wtb[:, 0:128], id_s[:, :], inc="wp_tr")
                    b.wait("scalar", "wp_tr")
                    copy("scalar", wtb[:, 128:256], ptr[:, 0:128], inc="wp_ev")
                    b.wait("sync", "wp_ev")
                    t = och * nin + ich
                    dma("sync", dst[t * 128:(t + 1) * 128, :], wtb[:, 128:256], inc="wp_st")
            b.wait("sync", "wp_st")

        def prep_direct(src, dst, nin, nout):
            for ich in range(nin):
                for och in range(nout):
                    b.wait("sync", "wd_st", b.cnt("wd_st"))
                    dma("sync", wtmp[:, 0:128], src[ich * 128:(ich + 1) * 128, och * 128:(och + 1) * 128], inc="wd_ld")
                    b.wait("vector", "wd_ld")
                    copy("vector", wtb[:, 0:128], wtmp[:, 0:128], inc="wd_c")
                    b.wait("sync", "wd_c")
                    t = ich * nout + och
                    dma("sync", dst[t * 128:(t + 1) * 128, :], wtb[:, 0:128], inc="wd_st")
            b.wait("sync", "wd_st")

        SET = ExitStack()
        ti_hold[0] = SET.enter_context(nc.sbuf_tensor("ti_set", [128, 1024], I32))
        wtmp = SET.enter_context(nc.sbuf_tensor("wtmp", [128, 512], F32))
        wtb = SET.enter_context(nc.sbuf_tensor("wtb", [128, 512], BF16))
        for och in range(0 if skip_prep else NCH):
            for half in range(2):
                b.wait("sync", "wa_c", b.cnt("wa_c"))
                dma("sync", wtmp[:, 0:512], dtt["Wffn0"][och * 128:(och + 1) * 128, half * 512:(half + 1) * 512], inc="wa_ld")
                b.wait("vector", "wa_ld")
                red(sm1[:, :], wtmp[:, 0:512], ALU.max, absval=True, inc="wa_c")
                tt("vector", amax_acc[:, :], amax_acc[:, :], sm1[:, :], ALU.max)
        b.emit("vector", lambda e: e.tensor_copy(sm1[0:1, :], sm1[0:1, :]), inc="wa_done")
        b.wait("tensor", "wa_done")
        pmax_col(sm1[0:1, 0:1], amax_acc[:, :], "wam")
        ts("vector", sm1[0:1, :], sm1[0:1, :], 1e-12, None, ALU.max, drain=True)
        recip(sm2[0:1, :], sm1[0:1, :])
        ts("vector", sm2[0:1, :], sm2[0:1, :], 448.0, None, ALU.mult, drain=True, inc="sxwv")
        b.wait("tensor", "sxwv")
        bcast11(sm2[0:1, 0:1], sxw_col[:, :], "sxwA")
        bcast11(sm1[0:1, 0:1], dqw_col[:, :], "sxwB")
        memset("vector", amax_acc[:, :], 0.0)

        if not skip_prep:
            prep_T(dtt["Wffn0"], dtt["wf0q_h"], NCH, NCH, quant="fp8", swcol=sxw_col)
            prep_T(dtt["Wqkv0"], dtt["wq0T_h"], 24, NCH)
            prep_T(dtt["Wproj0"], dtt["wp0T_h"], NCH, NCH)

        def mean_abs_scales(src, nout, ncols, swcol, dqcol, cccol, slot):
            memset("vector", aw_acc[:, :], 0.0)
            for och in range(nout):
                for blk5 in range(ncols // 512):
                    b.wait("sync", "ma_c", b.cnt("ma_c"))
                    dma("sync", wtmp[:, 0:512], src[och * 128:(och + 1) * 128, blk5 * 512:(blk5 + 1) * 512], inc="ma_ld")
                    b.wait("scalar", "ma_ld")
                    act(sq_s[:, 0:512], wtmp[:, 0:512], AF.Abs, inc="ma_ab")
                    b.wait("vector", "ma_ab")
                    b.emit("vector", lambda e: e.tensor_copy(sm4[0:1, :], sm4[0:1, :]), inc="ma_c")
                    red(sm1[:, :], sq_s[:, 0:512], ALU.add)
                    tt("vector", aw_acc[:, :], aw_acc[:, :], sm1[:, :], ALU.add)
            b.emit("vector", lambda e: e.tensor_copy(sm1[0:1, :], sm1[0:1, :]), inc="ma_d" + slot)
            b.wait("tensor", "ma_d" + slot)
            psum_col(sm1[0:1, 0:1], aw_acc[:, :], "mas" + slot)
            ts("vector", sm1[0:1, :], sm1[0:1, :], 1.0 / (DH * D), None, ALU.mult, drain=True)
            ts("vector", sm1[0:1, :], sm1[0:1, :], 1e-5, None, ALU.max, drain=True)
            recip(sm2[0:1, :], sm1[0:1, :], inc="mav" + slot)
            b.wait("tensor", "mav" + slot)
            bcast11(sm2[0:1, 0:1], swcol[:, :], "maA" + slot)
            bcast11(sm1[0:1, 0:1], dqcol[:, :], "maB" + slot)
            ts("vector", cccol[:, :], dqcol[:, :], 1.0 / 127.0, None, ALU.mult, drain=True)

        if not skip_prep:
            mean_abs_scales(dtt["w1"], NHC, D, sw1_col, dq1_col, cc1_col, "1")
            prep_T(dtt["w1"], dtt["w1q_h"], NHC, NCH, quant="tern", swcol=sw1_col)
            mean_abs_scales(dtt["w2"], NCH, DH, sw2_col, dq2_col, cc2_col, "2")
            prep_T(dtt["w2"], dtt["w2q_h"], NCH, NHC, quant="tern", swcol=sw2_col)
            prep_direct(dtt["Wqkv1"], dtt["wq1_h"], NCH, 24)
            prep_direct(dtt["Wdense1"], dtt["wd1_h"], NCH, NCH)
        SET.close()
        ti_hold[0] = None

        b.flush()
        if upto == "SETUP":
            return nc, dtt

        def ln_build_xnf(x_hbm, mu, r, xnf, eps, layer, amax=None):
            tg = f"L{layer}"
            b.wait("tensor", "ebr_c2")
            if layer == 1:
                b.wait("tensor", "P0tev")
            for t in range(NT):
                b.wait("sync", tg + "xnfd", b.cnt(tg + "xnfd"))
                b.wait("sync", tg + "xsu", b.cnt(tg + "xsu"))
                b.wait("sync", tg + "sq", b.cnt(tg + "sq"))
                dma("sync", xs[:, :], x_hbm[t * 128:(t + 1) * 128, :], inc=tg + "xld")
                b.wait("vector", tg + "xld")
                b.wait("scalar", tg + "xld")
                red(smA[:, :], xs[:, :], ALU.add)
                act(sq_s[:, 0:D], xs[:, :], AF.Square, inc=tg + "sqa")
                ts("vector", mu[:, t:t + 1], smA[:, :], 1.0 / D, None, ALU.mult, drain=True)
                b.wait("vector", tg + "sqa")
                red(smB[:, :], sq_s[:, 0:D], ALU.add, inc=tg + "sq")
                ts("vector", smC[:, :], smB[:, :], 1.0 / D, None, ALU.mult, drain=True)
                tt("vector", smD[:, :], mu[:, t:t + 1], mu[:, t:t + 1], ALU.mult, drain=True)
                tt("vector", smC[:, :], smC[:, :], smD[:, :], ALU.subtract, drain=True)
                ts("vector", smC[:, :], smC[:, :], eps, None, ALU.add, drain=True)
                rsqrt_col(r[:, t:t + 1], smC[:, :], tg + "rs")
                ts("vector", smD[:, :], mu[:, t:t + 1], -1.0, None, ALU.mult, drain=True)
                def apply_xn(e, a=smD, rr_=r, tt_=t):
                    e.tensor_scalar(xn_tm[:, :], xs[:, :], a[:, :], rr_[:, tt_:tt_ + 1],
                                    ALU.add, ALU.mult)
                    return e.drain()
                b.emit("vector", apply_xn, inc=tg + "xsu")
                if amax is not None:
                    red(smA[:, :], xn_tm[:, :], ALU.max, absval=True)
                    tt("vector", amax, amax, smA[:, :], ALU.max, drain=True)
                copy("vector", tb[:, 0:D], xn_tm[:, :], inc=tg + "xnc")
                b.wait("tensor", tg + "xnc")
                for ch in range(NCH):
                    b.wait("tensor", tg + "xnev", b.cnt(tg + "xnev"))
                    transp(ptr[:, 0:128], tb[:, ch * 128:(ch + 1) * 128], id_s[:, :], inc=tg + "xtr")
                    b.wait("scalar", tg + "xtr")
                    copy("scalar", xnf[ch][:, t * 128:(t + 1) * 128], ptr[:, 0:128], inc=tg + "xnev")
                b.emit("scalar", lambda e: e.copy(sq_s[0:1, 0:1], sq_s[0:1, 0:1]), inc=tg + "xnfd")
            b.wait("scalar", tg + "xnfd")
            b.emit("vector", lambda e: e.tensor_copy(smC[0:1, :], smC[0:1, :]), inc=tg + "vfin")

        def headln_evac(ps, kind, hch, sl, pstag, qf, kown):
            g = qg_s if kind == "q" else kg_s
            beta = qb_s if kind == "q" else kb_s
            copy("scalar", t0[:, 0:512], ps, inc="hl_e")
            b.wait("scalar", "hl_e")
            act(sq_s[:, 0:512], t0[:, 0:512], AF.Square, inc="hl_s")
            b.wait("tensor", "hl_s")
            b.wait("tensor", "hl_sv", b.cnt("hl_sv"))
            mm(pg0[0:2, 0:512], onesf2[:, :], t0[:, 0:512], True, True, inc="hl_m1")
            b.wait("vector", "hl_e")
            b.wait("vector", "hl_m1")
            ts("vector", t1[0:2, 0:512], pg0[0:2, 0:512], 1.0 / HD, None, ALU.mult, drain=True, inc="hl_e1")
            b.wait("tensor", "hl_e1")
            mm(pg0[0:2, 0:512], onesf2[:, :], sq_s[:, 0:512], True, True, inc="hl_m")
            b.wait("vector", "hl_m")
            ts("vector", t1[0:2, 512:1024], pg0[0:2, 0:512], 1.0 / HD, None, ALU.mult, drain=True, inc="hl_sv")
            tt("vector", t0[0:2, 512:1024], t1[0:2, 0:512], t1[0:2, 0:512], ALU.mult, drain=True)
            tt("vector", t0[0:2, 512:1024], t1[0:2, 512:1024], t0[0:2, 512:1024], ALU.subtract, drain=True)
            ts("vector", t0[0:2, 512:1024], t0[0:2, 512:1024], 1e-6, None, ALU.add, drain=True)
            rsqrt_col(sq_s[0:2, 512:1024], t0[0:2, 512:1024], "hlrs",
                      sc1=t1[0:2, 512:1024], sc2=sq_s[0:2, 0:512])
            tt("vector", t1[0:2, 0:512], t1[0:2, 0:512], sq_s[0:2, 512:1024], ALU.mult, drain=True, inc="hl_rw")
            b.wait("tensor", "hl_rw")
            b.wait("tensor", "hl_u", b.cnt("hl_u"))
            mm(pc0[:, 0:512], colind2[:, :], sq_s[0:2, 512:1024], True, True)
            mm(pc1[:, 0:512], colind2[:, :], t1[0:2, 0:512], True, True, inc="hl_RM")
            b.wait("vector", "hl_RM")
            tt("vector", t0[:, 0:512], t0[:, 0:512], pc0[:, 0:512], ALU.mult)
            tt("vector", t0[:, 0:512], t0[:, 0:512], pc1[:, 0:512], ALU.subtract, inc="hl_u")
            b.wait("scalar", "hl_u")
            if kind == "q":
                def affq(e, d=qf[hch], sl=sl):
                    return e.activation(d[:, sl * 512:(sl + 1) * 512], t0[:, 0:512],
                                        AF.Identity, scale=g[:, :], bias=beta[:, :])
                b.emit("scalar", affq, inc=pstag)
            else:
                rev = kown[hch][:, sl * 512:(sl + 1) * 512].rearrange(
                    "p (a c) -> p a c", c=128)[:, :, ::-1]
                def affk(e, d=rev):
                    return e.activation(d, t0[:, 0:512], AF.Identity, scale=g[:, :], bias=beta[:, :])
                b.emit("scalar", affk, inc=pstag)

        def qkv_phase(layer, xnf, qf, kown, tvf):
            tg = f"Q{layer}"
            wsrc = dtt["wq0T_h"] if layer == 0 else dtt["wq1_h"]
            lhsT_guard([tg + f"mmu{i}" for i in range(4)])
            b.wait("tensor", "ebr_c1")
            if layer == 1:
                b.wait("tensor", "A0exp")
                b.wait("tensor", "P0x1ev")
            ev_hist = {0: [], 1: []}
            for och in range(24):
                kind = "q" if och < 8 else ("k" if och < 16 else "v")
                hch = och % 8
                for sl in range(NSL):
                    it = och * NSL + sl
                    slot2 = it % 2
                    ps = scA[:, 0:512] if slot2 == 0 else scB[:, 0:512]
                    if len(ev_hist[slot2]) >= 1:
                        b.wait("tensor", tg + "ev", ev_hist[slot2][-1])
                    for ich in range(NCH):
                        t = (och * NCH + ich) if layer == 0 else (ich * 24 + och)
                        slot4 = (it * NCH + ich) % 4
                        ltag = tg + f"lb{slot4}"
                        mtag = tg + f"mmu{slot4}"
                        b.wait("sync", mtag, b.cnt(mtag))
                        dma("sync", lhsT[slot4][:, :], wsrc[t * 128:(t + 1) * 128, :], inc=ltag)
                        b.wait("tensor", ltag)
                        mm(ps, lhsT[slot4][:, :], xnf[ich][:, sl * 512:(sl + 1) * 512],
                           ich == 0, ich == NCH - 1, inc=mtag)
                    b.wait("scalar", tg + f"mmu{(it * NCH + NCH - 1) % 4}")
                    if layer == 0 and kind in ("q", "k"):
                        headln_evac(ps, kind, hch, sl, tg + "ev", qf, kown)
                    else:
                        bias = bqkv1_s[:, och:och + 1] if layer == 1 else None
                        if kind == "q":
                            def fwd(e, d=qf[hch], sl=sl, ps=ps, bi=bias):
                                kw = {"bias": bi} if bi is not None else {}
                                return e.activation(d[:, sl * 512:(sl + 1) * 512], ps,
                                                    AF.Identity, scale=1.0, **kw)
                            b.emit("scalar", fwd, inc=tg + "ev")
                        else:
                            dst = kown[hch] if kind == "k" else tvf[hch]
                            rev = dst[:, sl * 512:(sl + 1) * 512].rearrange(
                                "p (a c) -> p a c", c=128)[:, :, ::-1]
                            def bwd(e, d=rev, ps=ps, bi=bias):
                                kw = {"bias": bi} if bi is not None else {}
                                return e.activation(d, ps, AF.Identity, scale=1.0, **kw)
                            b.emit("scalar", bwd, inc=tg + "ev")
                    ev_hist[slot2].append(b.cnt(tg + "ev"))
            b.wait("scalar", tg + "ev")

        def v_stage(layer, tvf, vstage):
            tg = f"V{layer}"
            st_hist = {0: 0, 1: 0}
            for t in range(NT):
                sv = vstage[t % 2]
                if st_hist[t % 2]:
                    b.wait("tensor", tg + "st", st_hist[t % 2])
                for ch in range(NCH):
                    b.wait("tensor", tg + "vev", b.cnt(tg + "vev"))
                    transp(ptr[:, 0:128], tvf[ch][:, t * 128:(t + 1) * 128], id_s[:, :], inc=tg + "vtr")
                    b.wait("scalar", tg + "vtr")
                    copy("scalar", sv[:, ch * 130:ch * 130 + 64], ptr[:, 0:64], inc=tg + "vev")
                    copy("scalar", sv[:, ch * 130 + 65:ch * 130 + 129], ptr[:, 64:128], inc=tg + "vev")
                b.wait("gpsimd", tg + "vev")
                dma("gpsimd", mkap(dtt["ag_in"], N_K + (t * 128) * 1040, [(1040, 128), (1, 1040)]),
                    sv[:, :], inc=tg + "st")
                st_hist[t % 2] = b.cnt(tg + "st")
            b.wait("gpsimd", tg + "st")

        def k_stage(layer, kown):
            tg = f"K{layer}"
            for ch in range(NCH):
                dma("gpsimd", mkap(dtt["ag_in"], ch * 128 * T_OWN, [(T_OWN, 128), (1, T_OWN)]),
                    kown[ch][:, :], inc=tg + "st")
            b.wait("gpsimd", tg + "st")

        def exchange(layer, kf):
            tg = f"X{layer}"
            b.emit("gpsimd", lambda e: e.collective_compute(
                "AllGather", ALU.bypass, replica_groups=RG_PAIRS,
                ins=[dtt["ag_in"].opt()], outs=[dtt["ag_out"].opt()]), inc=tg + "cc")
            b.wait("gpsimd", tg + "cc")
            for s_ in range(2):
                for ch in range(NCH):
                    dma("gpsimd", kf[ch][:, s_ * T_OWN:(s_ + 1) * T_OWN],
                        mkap(dtt["ag_out"], s_ * SLOT + ch * 128 * T_OWN, [(T_OWN, 128), (1, T_OWN)]),
                        inc=tg + "kld")
            b.wait("gpsimd", tg + "kld")
            b.wait("sync", tg + "kld")
            b.wait("tensor", tg + "kld")
            b.wait("vector", tg + "kld")
            b.wait("scalar", tg + "kld")

        def negc_phase(layer, qf, kf):
            tg = f"C{layer}"
            for which, mat, ncols, dstm in (("q", qf, T_OWN, qm2), ("k", kf, S, km2)):
                for ch in range(NCH):
                    for sl in range(ncols // 512):
                        b.wait("scalar", tg + "nd", b.cnt(tg + "nd"))
                        act(sq_s[:, 0:512], mat[ch][:, sl * 512:(sl + 1) * 512], AF.Square, inc=tg + "ns")
                        b.wait("tensor", tg + "ns")
                        b.wait("tensor", tg + "nd", b.cnt(tg + "nd"))
                        mm(pg0[0:2, 0:512], onesf2[:, :], sq_s[:, 0:512], True, True, inc=tg + "nm")
                        b.wait("vector", tg + "nm")
                        red(sm1[0:2, :], pg0[0:2, 0:512], ALU.max)
                        if sl == 0:
                            copy("vector", dstm[:, ch:ch + 1], sm1[0:2, :], inc=tg + "nd")
                        else:
                            tt("vector", dstm[:, ch:ch + 1], dstm[:, ch:ch + 1], sm1[0:2, :],
                               ALU.max, drain=True, inc=tg + "nd")
            tt("vector", qm2[:, :], qm2[:, :], km2[:, :], ALU.mult, drain=True)
            ts("vector", qm2[:, :], qm2[:, :], 0.125 * 0.125, None, ALU.mult, drain=True, inc=tg + "c2")
            b.wait("tensor", tg + "c2")
            transp(pg0[0:8, 0:2], qm2[:, :], idf_s[0:2, 0:2], inc=tg + "ct")
            b.wait("vector", tg + "ct")
            red(sm1[0:8, :], pg0[0:8, 0:2], ALU.max, inc=tg + "cm")
            b.wait("tensor", tg + "cm")
            transp(pg0[0:1, 0:8], sm1[0:8, 0:1], idf_s[0:8, 0:8], inc=tg + "ctr2")
            b.wait("vector", tg + "ctr2")
            copy("vector", row512[0:1, 0:8], pg0[0:1, 0:8], inc=tg + "crw")
            b.wait("tensor", tg + "crw")
            bcast11(row512[0:1, 0:8], negC[:, 0:8], tg + "cb", w=8)
            b.wait("scalar", tg + "cb" + "bm", b.cnt(tg + "cb" + "bm"))
            ts("vector", negC[:, 0:8], negC[:, 0:8], 1e-30, None, ALU.max, drain=True, inc=tg + "ccl")
            b.wait("scalar", tg + "ccl")
            act(negC[:, 0:8], negC[:, 0:8], AF.Sqrt, inc=tg + "cs")
            b.wait("vector", tg + "cs")
            ts("vector", negC[:, 0:8], negC[:, 0:8], -1.0, None, ALU.mult, drain=True, inc=tg + "cn")
            b.wait("scalar", tg + "cn")

        def attention(layer, qf, kf, strips, vpair, probs, ctx_raw, recipb):
            tg = f"A{layer}"
            b.wait("tensor", f"Q{layer}ev")
            use_strip = layer == 1
            exp_hist = {0: [], 1: []}
            con_hist = {0: [], 1: []}
            pv_hist = []
            ctxev_hist = []
            vld_hist = {0: 0, 1: 0}
            for s_ in range(2):
                dma("gpsimd", vpair[0][:, s_ * 16 * 130:(s_ + 1) * 16 * 130],
                    mkap(dtt["ag_out"], s_ * SLOT + N_K + 0 * 130,
                         [(1040, 128), (1040 * 128, 16), (1, 130)]), inc=tg + "vl")
            vld_hist[0] = b.cnt(tg + "vl")
            for hp in range(NCH):
                vslot = hp % 2
                if hp + 1 < NCH:
                    nslot = (hp + 1) % 2
                    if hp >= 1 and len(pv_hist) >= 1:
                        b.wait("gpsimd", tg + "pv", pv_hist[-1])
                    for s_ in range(2):
                        dma("gpsimd", vpair[nslot][:, s_ * 16 * 130:(s_ + 1) * 16 * 130],
                            mkap(dtt["ag_out"], s_ * SLOT + N_K + (hp + 1) * 130,
                                 [(1040, 128), (1040 * 128, 16), (1, 130)]), inc=tg + "vl")
                    vld_hist[nslot] = b.cnt(tg + "vl")
                if use_strip:
                    for hh in range(2):
                        if hp >= 1:
                            b.wait("sync", tg + "pp", b.cnt(tg + "pp"))
                        dma("sync", strips[hh][:, :],
                            mkap(dtt["ebr_h"], (2 * hp + hh) * EBW, [(1, 128), (1, STRIPW)]),
                            inc=tg + f"sl{hh}")
                for qs in range(NSL):
                    for j in range(KCH):
                        slot2 = j % 2
                        ps = scA if slot2 == 0 else scB
                        if len(exp_hist[slot2]) >= 1:
                            b.wait("tensor", tg + "exp", exp_hist[slot2][-1])
                        if qs == 0 and j == 0:
                            b.wait("tensor", tg + "vl", vld_hist[vslot])
                        mm(ps[:, 0:512], kf[hp][0:64, j * 128:(j + 1) * 128],
                           qf[hp][0:64, qs * 512:(qs + 1) * 512], True, True, tp=(0, 0))
                        mm(ps[:, 512:1024], kf[hp][64:128, j * 128:(j + 1) * 128],
                           qf[hp][64:128, qs * 512:(qs + 1) * 512], True, True, tp=(64, 0),
                           inc=tg + "qk")
                        pb = probs[slot2]
                        b.wait("scalar", tg + "qk")
                        if len(con_hist[slot2]) >= 1:
                            b.wait("scalar",
                                   (tg + "pp") if use_strip else (tg + "pv"),
                                   con_hist[slot2][-1])
                        act(pb[:, :], ps[:, :], AF.Exp, bias=negC[:, hp:hp + 1],
                            scale=0.125, inc=tg + "exp")
                        exp_hist[slot2].append(b.cnt(tg + "exp"))
                        ptag = tg + "exp"
                        if use_strip:
                            b.wait("vector", tg + "exp")
                            if qs == 0 and j == 0:
                                b.wait("vector", tg + "sl0")
                                b.wait("vector", tg + "sl1")
                            u0 = 3968 + qs * 512 - j * 128
                            tt("vector", pb[:, 0:512], pb[:, 0:512], strips[0][:, u0:u0 + 512], ALU.mult)
                            tt("vector", pb[:, 512:1024], pb[:, 512:1024], strips[1][:, u0:u0 + 512],
                               ALU.mult, inc=tg + "pp")
                            ptag = tg + "pp"
                        b.wait("tensor", ptag)
                        if j == 0 and ctxev_hist:
                            b.wait("tensor", tg + "cev", ctxev_hist[-1])
                        mm(pc0[0:65, 0:512], vpair[vslot][:, j * 130:j * 130 + 65],
                           pb[:, 0:512], j == 0, j == KCH - 1)
                        mm(pc1[0:65, 0:512], vpair[vslot][:, j * 130 + 65:(j + 1) * 130],
                           pb[:, 512:1024], j == 0, j == KCH - 1, inc=tg + "pv")
                        pv_hist.append(b.cnt(tg + "pv"))
                        con_hist[slot2].append(b.cnt(tg + "pp" if use_strip else tg + "pv"))
                    b.wait("scalar", tg + "pv")
                    copy("scalar", ctx_raw[0:65, 0:512], pc0[0:65, :])
                    copy("scalar", ctx_raw[0:65, 512:1024], pc1[0:65, :], inc=tg + "cev")
                    ctxev_hist.append(b.cnt(tg + "cev"))
                    b.wait("vector", tg + "cev")
                    recip(ctx_raw[64:65, 0:512], ctx_raw[64:65, 0:512])
                    recip(ctx_raw[64:65, 512:1024], ctx_raw[64:65, 512:1024], inc=tg + "rcp")
                    b.wait("gpsimd", tg + "rcp")
                    dma("gpsimd", rr(dtt["scr_h"][512:1024], 512), ctx_raw[64:65, 0:512], inc=tg + "rw")
                    dma("gpsimd", rr(dtt["scr_h"][1024:1536], 512), ctx_raw[64:65, 512:1024], inc=tg + "rw")
                    b.wait("sync", tg + "rw")
                    b.wait("sync", tg + "cst", b.cnt(tg + "cst"))
                    dma("sync", recipb[0:64, 0:512], mkap(dtt["scr_h"], 512, [(0, 64), (1, 512)]), inc=tg + "rb")
                    dma("sync", recipb[0:64, 512:1024], mkap(dtt["scr_h"], 1024, [(0, 64), (1, 512)]), inc=tg + "rb")
                    b.wait("vector", tg + "rb")
                    tt("vector", tb[0:64, 0:512], ctx_raw[0:64, 0:512], recipb[0:64, 0:512], ALU.mult)
                    tt("vector", tb[0:64, 512:1024], ctx_raw[0:64, 512:1024], recipb[0:64, 512:1024],
                       ALU.mult, inc=tg + "ctxn")
                    b.wait("sync", tg + "ctxn")
                    dma("sync", dtt["ctx_h"][hp * 128:hp * 128 + 64, qs * 512:(qs + 1) * 512],
                        tb[0:64, 0:512], inc=tg + "cst")
                    dma("sync", dtt["ctx_h"][hp * 128 + 64:(hp + 1) * 128, qs * 512:(qs + 1) * 512],
                        tb[0:64, 512:1024], inc=tg + "cst")
            b.wait("sync", tg + "cst")
            b.wait("gpsimd", tg + "cst")

        with ExitStack() as SA:
            qf = [sb(SA, f"qf{i}", (128, T_OWN), BF16) for i in range(NCH)]

            with ExitStack() as SB:
                xnf = [sb(SB, f"xnf{i}", (128, T_OWN), BF16) for i in range(NCH)]
                kown = [sb(SB, f"kown{i}", (128, T_OWN), BF16) for i in range(NCH)]
                tvf = [sb(SB, f"tvf{i}", (128, T_OWN), BF16) for i in range(NCH)]
                vstage = [sb(SB, f"vst{i}", (128, 1040), BF16) for i in range(2)]

                for i_ in range(2):
                    memset("vector", vstage[i_][:, :], 0.0)
                for i_ in range(2):
                    for ch in range(NCH):
                        memset("vector", vstage[i_][:, ch * 130 + 64:ch * 130 + 65], 1.0)
                        memset("vector", vstage[i_][:, ch * 130 + 129:ch * 130 + 130], 1.0,
                               inc="vst_init")

                ln_build_xnf(dtt["x_own"], mu0, r0, xnf, 1e-5, 0, amax=amax_acc[:, :])
                tap("muearly", mu0[:, :], (128, NT), F32, prewait=("L0xnfd", "L0vfin"))
                b.emit("vector", lambda e: e.tensor_copy(sm1[0:1, :], sm1[0:1, :]), inc="amd")
                b.wait("tensor", "amd")
                pmax_col(sm1[0:1, 0:1], amax_acc[:, :], "amt")
                b.emit("vector", lambda e: e.tensor_copy(sm2[0:1, :], sm2[0:1, :]), inc="amv")
                b.wait("gpsimd", "amv")
                dma("gpsimd", dtt["ar_in"][:, :], sm1[0:1, 0:1], inc="amw")
                b.wait("gpsimd", "amw")
                b.emit("gpsimd", lambda e: e.collective_compute(
                    "AllReduce", ALU.max, replica_groups=[list(range(8))],
                    ins=[dtt["ar_in"].opt()], outs=[dtt["ar_out"].opt()]), inc="amcc")

                tap("xn0", xnf[0][:, 0:1024], (128, 1024), BF16, prewait=("L0xnfd", "L0vfin"))
                tap("xs15", xs[:, :], (128, 1024), F32, prewait=("L0xnfd", "L0vfin"))
                tap("sm2d", sm2[:, :], (128, 1), F32, prewait=("L0xnfd", "L0vfin"))
                tap("mu0", mu0[:, :], (128, NT), F32, prewait=("L0xnfd", "L0vfin"))
                tap("r0", r0[:, :], (128, NT), F32, prewait=("L0xnfd", "L0vfin"))
                b.flush()
                if upto == "P1":
                    return nc, dtt

                qkv_phase(0, xnf, qf, kown, tvf)
                b.wait("scalar", "vst_init")
                v_stage(0, tvf, vstage)
                k_stage(0, kown)
                tap("q0", qf[0][:, 0:1024], (128, 1024), BF16, prewait=("Q0ev",))
                tap("k0", kown[0][:, 0:1024], (128, 1024), BF16, prewait=("Q0ev",))
                tap("v0", tvf[0][:, 0:1024], (128, 1024), BF16, prewait=("Q0ev",))
                b.flush()
                if upto == "P2":
                    return nc, dtt

            with ExitStack() as SC:
                kf = [sb(SC, f"kf{i}", (128, S), BF16) for i in range(NCH)]
                vpair = [sb(SC, f"vp{i}", (128, 32 * 130), BF16) for i in range(2)]
                probs = [sb(SC, f"pr{i}", (128, 1024), BF16) for i in range(2)]
                ctx_raw = sb(SC, "ctxr_0", (128, 1024))
                recipb = sb(SC, "rcpb_0", (128, 1024))

                exchange(0, kf)
                negc_phase(0, qf, kf)
                tap("kf0", kf[0][:, 2048:3072], (128, 1024), BF16, prewait=("X0kld",))
                if "vchk" in taps:
                    outp("dbg_vchk", (128, 130), BF16)
                    b.wait("sync", "X0kld")
                    dma("sync", dtt["dbg_vchk"][:, :],
                        mkap(dtt["ag_out"], N_K + 0 * 130, [(1040, 128), (1, 130)]), inc="tapd")
                    b.wait("sync", "tapd")
                tap("negC0", negC[:, :], (128, 8), F32, prewait=("C0cn",))
                b.flush()
                if upto == "P3":
                    return nc, dtt

                attention(0, qf, kf, None, vpair, probs, ctx_raw, recipb)
                tap("ctx0", ctx_raw[:, :], (128, 1024), F32, prewait=("A0cst",))
                tap("vp1", vpair[1][:, 0:260], (128, 260), BF16, prewait=("A0cst",))
                tap("pr0", probs[0][:, :], (128, 1024), BF16, prewait=("A0cst",))
                b.flush()
                if upto == "P4":
                    return nc, dtt

        b.wait("sync", "amcc")
        dma("sync", sm1[0:1, 0:1], dtt["ar_out"][:, :], inc="amr")
        b.wait("vector", "amr")
        ts("vector", sm1[0:1, :], sm1[0:1, :], 1e-12, None, ALU.max)
        recip(sm2[0:1, :], sm1[0:1, :])
        ts("vector", sm2[0:1, :], sm2[0:1, :], 448.0, None, ALU.mult, drain=True, inc="amv2")
        b.wait("tensor", "amv2")
        bcast11(sm2[0:1, 0:1], sxq_col[:, :], "amA")
        bcast11(sm1[0:1, 0:1], scg_col[:, :], "amB")
        tt("vector", scg_col[:, :], scg_col[:, :], dqw_col[:, :], ALU.mult, drain=True)
        ts("vector", scg_col[:, :], scg_col[:, :], 1.0 / (448.0 * 448.0), None, ALU.mult, drain=True, inc="scgv")
        b.wait("scalar", "scgv")

        def proj_ffn_layer(layer):
            tg = f"P{layer}"
            wproj = dtt["wp0T_h"] if layer == 0 else dtt["wd1_h"]
            bias_p = bproj_s if layer == 0 else bdense_s
            b.wait("tensor", f"A{layer}cev")
            lhsT_guard([tg + f"pm{i}" for i in range(4)])
            with ExitStack() as SD:
                ctx_sl = [sb(SD, f"ctxs{i}_{layer}", (128, 512), BF16) for i in range(NCH)]
                attn_sb = [sb(SD, f"atn{i}_{layer}", (128, 512)) for i in range(NCH)]
                xqf = ctx_sl
                xf = [sb(SD, f"xf{i}_{layer}", (128, 512)) for i in range(NCH)]
                ti_hold[0] = SD.enter_context(nc.sbuf_tensor(f"ti_p{layer}", [128, 1024], I32))
                x1tm = [sb(SD, f"x1tm{i}_{layer}", (128, 1024)) for i in range(4)]
                if layer == 1:
                    h2b_blk = [sb(SD, f"h2bB{i}", (128, 2048)) for i in range(8)]
                    h2b = [h2b_blk[i // 4][:, (i % 4) * 512:(i % 4 + 1) * 512] for i in range(NHC)]
                    h2q_blk = [sb(SD, f"h2qB{i}", (128, 2048), BF16) for i in range(8)]
                    h2q = [h2q_blk[i // 4][:, (i % 4) * 512:(i % 4 + 1) * 512] for i in range(NHC)]
                    ffn_layer1_edges(xqf, h2b)

                for sl in range(NSL):
                    for ch in range(NCH):
                        b.wait("sync", tg + "pmm", b.cnt(tg + "pmm"))
                        dma("sync", ctx_sl[ch][:, :],
                            dtt["ctx_h"][ch * 128:(ch + 1) * 128, sl * 512:(sl + 1) * 512],
                            inc=tg + "cld")
                    ev_hist = []
                    for och in range(NCH):
                        psl = pg0 if och % 2 == 0 else pc1
                        if len(ev_hist) >= 2:
                            b.wait("tensor", tg + "pev", ev_hist[-2])
                        for ich in range(NCH):
                            t = (och * NCH + ich) if layer == 0 else (ich * NCH + och)
                            slot4 = (och * NCH + ich) % 4
                            ltag = tg + f"pl{slot4}"
                            b.wait("sync", tg + f"pm{slot4}", b.cnt(tg + f"pm{slot4}"))
                            dma("sync", lhsT[slot4][:, :], wproj[t * 128:(t + 1) * 128, :], inc=ltag)
                            b.wait("tensor", ltag)
                            if ich == 0:
                                b.wait("tensor", tg + "cld")
                            mm(psl[:, :], lhsT[slot4][:, :], ctx_sl[ich][:, :],
                               ich == 0, ich == NCH - 1, inc=tg + f"pm{slot4}")
                        b.wait("scalar", tg + f"pm{(och * NCH + NCH - 1) % 4}")
                        def pev(e, d=attn_sb[och], ps=psl, bi=bias_p[:, och:och + 1]):
                            return e.activation(d[:, :], ps[:, :], AF.Identity, scale=1.0, bias=bi)
                        b.emit("scalar", pev, inc=tg + "pev")
                        ev_hist.append(b.cnt(tg + "pev"))
                    b.wait("tensor", tg + "pev")
                    if layer == 0 and sl == 0 and "atn0" in taps:
                        outp("dbg_atn0", (128, 512))
                        b.wait("sync", tg + "pev")
                        dma("sync", dtt["dbg_atn0"][:, :], attn_sb[0][:, :], inc="tapd")
                        b.wait("sync", "tapd")
                    if layer == 1 and sl == 0 and "atn1" in taps:
                        outp("dbg_atn1", (128, 512))
                        b.wait("sync", tg + "pev")
                        dma("sync", dtt["dbg_atn1"][:, :], attn_sb[0][:, :], inc="tapd")
                        b.wait("sync", "tapd")


                    xsrc = dtt["x_own"] if layer == 0 else dtt["x1_h"]
                    mu, rv = (mu0, r0) if layer == 0 else (mu1, r1)
                    for t4 in range(4):
                        tix = sl * 4 + t4
                        b.wait("sync", tg + "xcons", b.cnt(tg + "xcons"))
                        dma("sync", xs[:, :], xsrc[tix * 128:(tix + 1) * 128, :], inc=tg + "x4")
                        b.wait("vector", tg + "x4")
                        ts("vector", sm4[:, :], mu[:, tix:tix + 1], -1.0, None, ALU.mult, drain=True)
                        b.emit("vector", lambda e, a=sm4, r_=rv, tx=tix:
                               e.tensor_scalar(xn_tm[:, :], xs[:, :], a[:, :], r_[:, tx:tx + 1],
                                               ALU.add, ALU.mult), inc=tg + "xnb")
                        if layer == 0:
                            fp8q(tb[:, 0:1024], xn_tm[:, :], 1024, sxq_col)
                            b.emit("vector", lambda e: e.tensor_copy(sm1[0:1, :], sm1[0:1, :]),
                                   inc=tg + "qz")
                        else:
                            red(sm1[:, :], xn_tm[:, :], ALU.max, absval=True)
                            ts("vector", sm1[:, :], sm1[:, :], 1e-5, None, ALU.max, drain=True)
                            recip(sm2[:, :], sm1[:, :])
                            ts("vector", sm2[:, :], sm2[:, :], 127.0, None, ALU.mult, drain=True)
                            i8q_col(tb[:, 0:1024], xn_tm[:, :], 1024, sm2[:, :])
                            tt("vector", sm3[:, :], sm1[:, :], cc1_col[:, :], ALU.mult, drain=True, inc=tg + "qz")
                            b.wait("gpsimd", tg + "qz")
                            dma("gpsimd", rr(dtt["scr_h"][1536 + t4 * 128:1536 + (t4 + 1) * 128], 1),
                                sm3[:, :], inc=tg + "dqw")
                        b.wait("tensor", tg + "qz")
                        for ch in range(NCH):
                            b.wait("tensor", tg + "tev", b.cnt(tg + "tev"))
                            transp(ptr[:, 0:128], tb[:, ch * 128:(ch + 1) * 128], id_s[:, :], inc=tg + "ttr")
                            b.wait("scalar", tg + "ttr")
                            copy("scalar", xqf[ch][:, t4 * 128:(t4 + 1) * 128], ptr[:, 0:128], inc=tg + "tev")
                        copy("vector", sq_s[:, 0:1024], xs[:, :], inc=tg + "xc")
                        b.wait("tensor", tg + "xc")
                        for ch in range(NCH):
                            b.wait("tensor", tg + "fev", b.cnt(tg + "fev"))
                            transp(pc0[:, 128:256], sq_s[:, ch * 128:(ch + 1) * 128], idf_s[:, :], inc=tg + "ftr")
                            b.wait("scalar", tg + "ftr")
                            copy("scalar", xf[ch][:, t4 * 128:(t4 + 1) * 128], pc0[:, 128:256], inc=tg + "fev")
                        b.emit("scalar", lambda e: e.copy(sm4[0:1, :], sm4[0:1, :]), inc=tg + "xcons")

                    if layer == 0:
                        b.wait("tensor", tg + "tev")
                        b.wait("tensor", tg + "fev")
                        lhsT_guard([tg + f"fm{i}" for i in range(4)])
                        ev2 = []
                        for och in range(NCH):
                            psl = pg0 if och % 2 == 0 else pc1
                            if len(ev2) >= 2:
                                b.wait("tensor", tg + "fe2", ev2[-2])
                            for ich in range(NCH):
                                t = och * NCH + ich
                                slot4 = (och * NCH + ich) % 4
                                ltag = tg + f"fl{slot4}"
                                b.wait("sync", tg + f"fm{slot4}", b.cnt(tg + f"fm{slot4}"))
                                dma("sync", lhsT[slot4][:, :], dtt["wf0q_h"][t * 128:(t + 1) * 128, :], inc=ltag)
                                b.wait("tensor", ltag)
                                mm(psl[:, :], lhsT[slot4][:, :], xqf[ich][:, :],
                                   ich == 0, ich == NCH - 1, inc=tg + f"fm{slot4}")
                            b.wait("scalar", tg + f"fm{(och * NCH + NCH - 1) % 4}")
                            def fev(e, ps=psl, bi=bffn_s[:, och:och + 1]):
                                return e.activation(t1[:, 0:512], ps[:, :], AF.Identity,
                                                    scale=scg_col[:, :], bias=bi)
                            b.emit("scalar", fev, inc=tg + "fe2")
                            ev2.append(b.cnt(tg + "fe2"))
                            b.wait("vector", tg + "fe2")
                            tt("vector", t1[:, 0:512], t1[:, 0:512], attn_sb[och][:, :], ALU.add)
                            tt("vector", t1[:, 0:512], t1[:, 0:512], xf[och][:, :], ALU.add, inc=tg + "x1c")
                            b.wait("sync", tg + "x1c")
                            dma("sync", dtt["x1f_h"][och * 128:(och + 1) * 128, sl * 512:(sl + 1) * 512],
                                t1[:, 0:512], inc=tg + "x1fst")
                            b.wait("tensor", tg + "x1c")
                            for t4 in range(4):
                                b.wait("tensor", tg + "x1ev", b.cnt(tg + "x1ev"))
                                transp(pc0[:, 0:128], t1[:, t4 * 128:(t4 + 1) * 128],
                                       idf_s[:, :], inc=tg + "x1tr")
                                b.wait("scalar", tg + "x1tr")
                                copy("scalar", x1tm[t4][:, och * 128:(och + 1) * 128],
                                     pc0[:, 0:128], inc=tg + "x1ev")
                        b.wait("sync", tg + "x1ev")
                        for t4 in range(4):
                            tix = sl * 4 + t4
                            dma("sync", dtt["x1_h"][tix * 128:(tix + 1) * 128, :], x1tm[t4][:, :],
                                inc=tg + "x1st")
                    else:
                        ffn_layer1_quarter(sl, xqf, xf, attn_sb, h2b, h2q, x1tm)
                        if sl == 0 and "h2b0" in taps:
                            outp("dbg_h2b0", (128, 512))
                            outp("dbg_sct0", (128, 512))
                            outp("dbg_sct1", (128, 512))
                            outp("dbg_xq1", (128, 512), BF16)
                            b.wait("sync", "Ff")
                            b.wait("sync", "xev")
                            dma("sync", dtt["dbg_h2b0"][:, :], h2b[0], inc="tapd")
                            dma("sync", dtt["dbg_sct0"][:, :], sct0[:, :], inc="tapd")
                            dma("sync", dtt["dbg_sct1"][:, :], sct1[:, :], inc="tapd")
                            dma("sync", dtt["dbg_xq1"][:, :], xqf[0][:, :], inc="tapd")
                            b.wait("sync", "tapd")
                if layer == 0:
                    b.wait("sync", tg + "x1st")
                    b.wait("sync", tg + "x1fst")

        def ffn_layer1_edges(xqf, h2b):
            tg = "E1"
            lhsT_guard([tg + f"em{i}" for i in range(4)])
            dma("sync", rowB0[0:1, :], mkap(dtt["ag_out"], N_K + N_V + D, [(0, 1), (1, D)]), inc=tg + "hld")
            dma("sync", rowB1[0:1, :], mkap(dtt["ag_out"], SLOT + N_K + N_V, [(0, 1), (1, D)]), inc=tg + "hld")
            b.wait("vector", tg + "hld")
            copy("vector", haloL[0:1, :], rowB0[0:1, :])
            copy("vector", haloR[0:1, :], rowB1[0:1, :])
            ts("vector", haloL[0:1, :], haloL[0:1, :], hm_s[0:1, 0:1], None, ALU.mult)
            ts("vector", haloR[0:1, :], haloR[0:1, :], hm_s[0:1, 1:2], None, ALU.mult, inc=tg + "hm")
            for e_ in range(5):
                if e_ == 0:
                    src = haloL[0:1, :]
                elif e_ == 4:
                    src = haloR[0:1, :]
                else:
                    src = None
                if src is None:
                    dma("sync", qrow[:, :], dtt["x1_h"][e_ * 512:e_ * 512 + 1, :], inc=tg + "xr")
                    b.wait("vector", tg + "xr")
                else:
                    b.wait("vector", tg + "hm")
                    copy("vector", qrow[:, :], src)
                red(sm1[0:1, :], qrow[:, :], ALU.add)
                ts("vector", sm1[0:1, :], sm1[0:1, :], 1.0 / D, None, ALU.mult, drain=True)
                tt("vector", t0[0:1, 0:D], qrow[:, :], qrow[:, :], ALU.mult)
                red(sm2[0:1, :], t0[0:1, 0:D], ALU.add)
                ts("vector", sm2[0:1, :], sm2[0:1, :], 1.0 / D, None, ALU.mult, drain=True)
                tt("vector", sm3[0:1, :], sm1[0:1, :], sm1[0:1, :], ALU.mult, drain=True)
                tt("vector", sm2[0:1, :], sm2[0:1, :], sm3[0:1, :], ALU.subtract, drain=True)
                ts("vector", sm2[0:1, :], sm2[0:1, :], 1e-5, None, ALU.add, drain=True)
                recip(sm3[0:1, :], sm2[0:1, :], inc=tg + "rv")
                b.wait("scalar", tg + "rv")
                act(sm3[0:1, 0:1], sm3[0:1, 0:1], AF.Sqrt, inc=tg + "rt")
                b.wait("vector", tg + "rt")
                ts("vector", sm4[0:1, :], sm1[0:1, :], -1.0, None, ALU.mult, drain=True)
                b.emit("vector", lambda e: e.tensor_scalar(
                    t0[0:1, 0:D], qrow[:, :], sm4[0:1, :], sm3[0:1, 0:1], ALU.add, ALU.mult))
                red(sm1[0:1, :], t0[0:1, 0:D], ALU.max, absval=True)
                ts("vector", sm1[0:1, :], sm1[0:1, :], 1e-5, None, ALU.max, drain=True)
                recip(sm2[0:1, :], sm1[0:1, :])
                ts("vector", sm2[0:1, :], sm2[0:1, :], 127.0, None, ALU.mult, drain=True)
                b.emit("vector", lambda e: e.tensor_scalar(
                    t0[0:1, 0:D], t0[0:1, 0:D], sm2[0:1, 0:1], None, ALU.mult))
                copy("vector", ti_hold[0][0:1, 0:D], t0[0:1, 0:D])
                copy("vector", t0[0:1, 0:D], ti_hold[0][0:1, 0:D])
                ts("vector", t0[0:1, 0:D], t0[0:1, 0:D], -128.0, 127.0, ALU.max, ALU.min)
                copy("vector", tb[0:1, 0:D], t0[0:1, 0:D], inc=tg + "xq")
                tt("vector", sm3[0:1, :], sm1[0:1, :], cc1_col[0:1, 0:1], ALU.mult, drain=True, inc=tg + "sc")
                b.wait("tensor", tg + "sc")
                bcast11(sm3[0:1, 0:1], sm4[:, :], tg + "scB")
                b.emit("vector", lambda e: e.tensor_copy(sm3[0:1, :], sm3[0:1, :]), inc=tg + "scr")
                b.wait("tensor", tg + "xq")
                for ch in range(NCH):
                    b.wait("tensor", tg + "qc", b.cnt(tg + "qc"))
                    mm(pg0[:, 300:301], tb[0:1, ch * 128:(ch + 1) * 128], onesb1[0:1, 0:1],
                       True, True, inc=tg + "qtr")
                    b.wait("scalar", tg + "qtr")
                    copy("scalar", qcol[:, ch:ch + 1], pg0[:, 300:301], inc=tg + "qc")
                b.wait("tensor", tg + "qc")
                b.wait("vector", tg + "scr")
                for ch in range(NHC):
                    b.wait("tensor", tg + "ec", b.cnt(tg + "ec"))
                    for ich in range(NCH):
                        t = ch * NCH + ich
                        slot4 = (ch * NCH + ich) % 4
                        ltag = tg + f"el{slot4}"
                        b.wait("sync", tg + f"em{slot4}", b.cnt(tg + f"em{slot4}"))
                        dma("sync", lhsT[slot4][:, :], dtt["w1q_h"][t * 128:(t + 1) * 128, :], inc=ltag)
                        b.wait("tensor", ltag)
                        mm(pg0[:, 256:257], lhsT[slot4][:, :], qcol[:, ich:ich + 1],
                           ich == 0, ich == NCH - 1, inc=tg + f"em{slot4}")
                    b.wait("vector", tg + f"em{(ch * NCH + NCH - 1) % 4}")
                    tt("vector", hedge5[:, e_ * NHC + ch:e_ * NHC + ch + 1], pg0[:, 256:257],
                       sm4[:, :], ALU.mult, drain=True, inc=tg + "ec")
            b.wait("vector", tg + "ec")

        def ffn_layer1_quarter(sl, xqf, xf, attn_sb, h2b, h2q, x1tm):
            tg = f"F{sl}"
            b.wait("sync", "P1dqw")
            dma("sync", sct0[:, :], mkap(dtt["scr_h"], 1536, [(0, 128), (1, 512)]), inc=tg + "dq")
            b.wait("vector", tg + "dq")
            memset("vector", sq_s[:, 0:512], 0.0, inc=tg + "hm0")
            b.wait("tensor", "P1tev")
            b.wait("tensor", "P1fev")
            lhsT_guard([tg + f"wm{i}" for i in range(4)])
            evh = []
            for ch in range(NHC):
                psl = pg0 if ch % 2 == 0 else pc1
                if len(evh) >= 2:
                    b.wait("tensor", tg + "hev", evh[-2])
                for ich in range(NCH):
                    t = ch * NCH + ich
                    slot4 = (ch * NCH + ich) % 4
                    ltag = tg + f"wl{slot4}"
                    b.wait("sync", tg + f"wm{slot4}", b.cnt(tg + f"wm{slot4}"))
                    dma("sync", lhsT[slot4][:, :], dtt["w1q_h"][t * 128:(t + 1) * 128, :], inc=ltag)
                    b.wait("tensor", ltag)
                    mm(psl[:, :], lhsT[slot4][:, :], xqf[ich][:, :],
                       ich == 0, ich == NCH - 1, inc=tg + f"wm{slot4}")
                b.wait("vector", tg + f"wm{(ch * NCH + NCH - 1) % 4}")
                tt("vector", hbuf[:, 1:513], psl[:, :], sct0[:, :], ALU.mult, inc=tg + "hev")
                evh.append(b.cnt(tg + "hev"))
                if sl == 0 and ch == 0 and "hb0" in taps:
                    outp("dbg_hb0", (128, 514))
                    outp("dbg_dq0", (128, 512))
                    outp("dbg_sw1", (128, 4))
                    outp("dbg_w1t", (256, 128), BF16)
                    b.wait("sync", tg + "hev")
                    dma("sync", dtt["dbg_hb0"][:, :], hbuf[:, 0:514], inc="tapd")
                    dma("sync", dtt["dbg_dq0"][:, :], sct0[:, :], inc="tapd")
                    dma("sync", dtt["dbg_sw1"][:, 0:1], sw1_col[:, :], inc="tapd", slow=True)
                    dma("sync", dtt["dbg_sw1"][:, 1:2], dq1_col[:, :], inc="tapd", slow=True)
                    dma("sync", dtt["dbg_sw1"][:, 2:3], cc1_col[:, :], inc="tapd", slow=True)
                    dma("sync", dtt["dbg_w1t"][:, :], dtt["w1q_h"][0:256, :], inc="tapd")
                    b.wait("sync", "tapd")
                lsrc = hedge5[:, 0 * NHC + ch:0 * NHC + ch + 1] if sl == 0 else hedgeL[:, ch:ch + 1]
                b.emit("vector", lambda e, d=hbuf[:, 0:1], i=lsrc:
                       (e.tensor_copy(d, i), e.drain())[-1])
                ridx = 4 if sl == 3 else sl + 1
                b.emit("vector", lambda e, d=hbuf[:, 513:514],
                       i=hedge5[:, ridx * NHC + ch:ridx * NHC + ch + 1]:
                       (e.tensor_copy(d, i), e.drain())[-1], inc=tg + "hed")
                ts("vector", t0[:, 0:512], hbuf[:, 0:512], convw_s[:, ch * 3:ch * 3 + 1], None, ALU.mult)
                stt("vector", t0[:, 0:512], hbuf[:, 1:513], convw_s[:, ch * 3 + 1:ch * 3 + 2],
                    t0[:, 0:512], ALU.mult, ALU.add)
                stt("vector", t0[:, 0:512], hbuf[:, 2:514], convw_s[:, ch * 3 + 2:ch * 3 + 3],
                    t0[:, 0:512], ALU.mult, ALU.add, inc=tg + "cv")
                if sl < 3:
                    copy("vector", hedgeL[:, ch:ch + 1], hbuf[:, 512:513])
                b.wait("scalar", tg + "cv")
                act(h2b[ch], t0[:, 0:512], AF.Silu, bias=convb_s[:, ch:ch + 1])
                act(t1[:, 512:1024], h2b[ch], AF.Abs, inc=tg + "si")
                b.wait("vector", tg + "si")
                tt("vector", sq_s[:, 0:512], sq_s[:, 0:512], t1[:, 512:1024], ALU.max, inc=tg + "hmx")
            b.wait("tensor", tg + "hmx")
            for t4 in range(4):
                b.wait("tensor", tg + "amv", b.cnt(tg + "amv"))
                transp(pg0[:, 128:256], sq_s[:, t4 * 128:(t4 + 1) * 128], idf_s[:, :], inc=tg + "amt")
                b.wait("vector", tg + "amt")
                red(amax4[:, t4:t4 + 1], pg0[:, 128:256], ALU.max, inc=tg + "amv")
            ts("vector", amax4[:, :], amax4[:, :], 1e-5, None, ALU.max, drain=True)
            recip(deq4[:, 0:4], amax4[:, :])
            ts("vector", deq4[:, 0:4], deq4[:, 0:4], 127.0, None, ALU.mult, drain=True, inc=tg + "qs")
            ts("vector", t1[:, 0:4], amax4[:, :], cc2_col[:, :], None, ALU.mult,
               drain=True, inc=tg + "d2")
            b.wait("gpsimd", tg + "qs")
            b.wait("gpsimd", tg + "d2")
            for t4 in range(4):
                dma("gpsimd", rr(dtt["scr_h"][2304 + t4 * 128:2304 + (t4 + 1) * 128], 1),
                    deq4[:, t4:t4 + 1], inc=tg + "qsw")
                dma("gpsimd", rr(dtt["scr_h"][2816 + t4 * 128:2816 + (t4 + 1) * 128], 1),
                    t1[:, t4:t4 + 1], inc=tg + "qsw")
            b.wait("sync", tg + "qsw")
            dma("sync", sct0[:, :], mkap(dtt["scr_h"], 2304, [(0, 128), (1, 512)]), inc=tg + "qsr")
            dma("sync", sct1[:, :], mkap(dtt["scr_h"], 2816, [(0, 128), (1, 512)]), inc=tg + "qsr")
            b.wait("vector", tg + "qsr")
            for ch in range(NHC):
                i8q_tile(h2q[ch], h2b[ch], 512, sct0[:, :])
            b.emit("vector", lambda e: e.tensor_copy(sm1[0:1, :], sm1[0:1, :]), inc=tg + "qd")
            b.wait("tensor", tg + "qd")
            lhsT_guard([tg + f"zm{i}" for i in range(4)])
            evz = []
            for och in range(NCH):
                psl = pg0 if och % 2 == 0 else pc1
                if len(evz) >= 2:
                    b.wait("tensor", tg + "zev", evz[-2])
                for ch in range(NHC):
                    t = och * NHC + ch
                    slot4 = (och * NHC + ch) % 4
                    ltag = tg + f"zl{slot4}"
                    b.wait("sync", tg + f"zm{slot4}", b.cnt(tg + f"zm{slot4}"))
                    dma("sync", lhsT[slot4][:, :], dtt["w2q_h"][t * 128:(t + 1) * 128, :], inc=ltag)
                    b.wait("tensor", ltag)
                    mm(psl[:, :], lhsT[slot4][:, :], h2q[ch],
                       ch == 0, ch == NHC - 1, inc=tg + f"zm{slot4}")
                b.wait("vector", tg + f"zm{(och * NHC + NHC - 1) % 4}")
                tt("vector", t1[:, 0:512], psl[:, :], sct1[:, :], ALU.mult, inc=tg + "zev")
                evz.append(b.cnt(tg + "zev"))
                tt("vector", t1[:, 0:512], t1[:, 0:512], attn_sb[och][:, :], ALU.add)
                tt("vector", t1[:, 0:512], t1[:, 0:512], xf[och][:, :], ALU.add, inc=tg + "x2c")
                b.wait("tensor", tg + "x2c")
                for t4 in range(4):
                    b.wait("tensor", tg + "x2ev", b.cnt(tg + "x2ev"))
                    transp(pc0[:, 128:256], t1[:, t4 * 128:(t4 + 1) * 128],
                           idf_s[:, :], inc=tg + "x2tr")
                    b.wait("scalar", tg + "x2tr")
                    copy("scalar", x1tm[t4][:, och * 128:(och + 1) * 128],
                         pc0[:, 128:256], inc=tg + "x2ev")
            b.wait("sync", tg + "x2ev")
            for t4 in range(4):
                tix = sl * 4 + t4
                dma("sync", dtt["y"][tix * 128:(tix + 1) * 128, :], x1tm[t4][:, :], inc="yout")

        proj_ffn_layer(0)
        tap("x1", dtt["x1_h"][0:128, :], (128, 1024), F32, prewait=("P0x1st",))
        tap("x1f", dtt["x1f_h"][0:128, 0:512], (128, 512), F32, prewait=("P0x1st",))
        b.flush()
        if upto == "P5":
            return nc, dtt

        with ExitStack() as SA:
            qf = [sb(SA, f"qg{i}", (128, T_OWN), BF16) for i in range(NCH)]
            with ExitStack() as SB:
                xnf = [sb(SB, f"xng{i}", (128, T_OWN), BF16) for i in range(NCH)]
                kown = [sb(SB, f"kog{i}", (128, T_OWN), BF16) for i in range(NCH)]
                tvf = [sb(SB, f"tvg{i}", (128, T_OWN), BF16) for i in range(NCH)]
                vstage = [sb(SB, f"vsg{i}", (128, 1040), BF16) for i in range(2)]
                for i_ in range(2):
                    memset("vector", vstage[i_][:, :], 0.0)
                for i_ in range(2):
                    for ch in range(NCH):
                        memset("vector", vstage[i_][:, ch * 130 + 64:ch * 130 + 65], 1.0)
                        memset("vector", vstage[i_][:, ch * 130 + 129:ch * 130 + 130], 1.0,
                               inc="vst1")
                ln_build_xnf(dtt["x1_h"], mu1, r1, xnf, 1e-5, 1)
                tap("xn1", xnf[0][:, 0:1024], (128, 1024), BF16, prewait=("L1xnfd", "L1vfin"))
                b.flush()
                if upto == "P6a":
                    return nc, dtt
                qkv_phase(1, xnf, qf, kown, tvf)
                b.wait("scalar", "vst1")
                v_stage(1, tvf, vstage)
                k_stage(1, kown)
                b.wait("sync", "P0x1st", b.cnt("P0x1st"))
                dma("sync", qrow[:, :], dtt["x1_h"][0:1, :], inc="hb1")
                b.wait("vector", "hb1")
                copy("vector", rowB0[0:1, 0:D], qrow[:, :], inc="hb1c")
                dma("sync", qrow[:, :], dtt["x1_h"][T_OWN - 1:T_OWN, :], inc="hb2")
                b.wait("vector", "hb2")
                copy("vector", rowB1[0:1, 0:D], qrow[:, :], inc="hb2c")
                b.wait("gpsimd", "hb1c")
                b.wait("gpsimd", "hb2c")
                dma("gpsimd", mkap(dtt["ag_in"], N_K + N_V, [(0, 1), (1, D)]), rowB0[0:1, 0:D], inc="hbst")
                dma("gpsimd", mkap(dtt["ag_in"], N_K + N_V + D, [(0, 1), (1, D)]), rowB1[0:1, 0:D], inc="hbst")
                b.wait("gpsimd", "hbst")
                b.flush()
                if upto == "P6":
                    return nc, dtt
            with ExitStack() as SC:
                kf = [sb(SC, f"kg{i}", (128, S), BF16) for i in range(NCH)]
                vpair = [sb(SC, f"vq{i}", (128, 32 * 130), BF16) for i in range(2)]
                probs = [sb(SC, f"pq{i}", (128, 1024), BF16) for i in range(2)]
                strips = [sb(SC, f"sq{i}", (128, STRIPW), BF16) for i in range(2)]
                ctx_raw = sb(SC, "ctxr_1", (128, 1024))
                recipb = sb(SC, "rcpb_1", (128, 1024))
                exchange(1, kf)
                negc_phase(1, qf, kf)
                b.flush()
                if upto == "P7":
                    return nc, dtt
                attention(1, qf, kf, strips, vpair, probs, ctx_raw, recipb)
                tap("ctx1", ctx_raw[:, :], (128, 1024), F32, prewait=("A1cst",))
                tap("q1", qf[0][:, 0:1024], (128, 1024), BF16, prewait=("Q1ev",))
                tap("k1f", kf[0][:, 0:1024], (128, 1024), BF16, prewait=("X1kld",))
                tap("strip1", strips[0][:, 0:1024], (128, 1024), BF16, prewait=("A1cst",))
                b.flush()
                if upto == "P8":
                    return nc, dtt
        proj_ffn_layer(1)
        b.wait("sync", "yout")
        b.flush()

    return nc, dtt


def _host_inputs(inputs):
    x = np.asarray(inputs["L0_attn_x"], dtype=np.float32)
    per_core = []
    ident_b = np.eye(128, dtype=ml_dtypes.bfloat16)
    ident_f = np.eye(128, dtype=np.float32)
    colind2 = np.zeros((2, 128), np.float32)
    colind2[0, 0:64] = 1.0
    colind2[1, 64:128] = 1.0
    tabT = np.ascontiguousarray(np.asarray(inputs["L1_attn_rel_attn_table"], np.float32).T)
    convw = np.ascontiguousarray(np.asarray(inputs["L1_ffn_conv_w"], np.float32)[:, 0, :])
    for c in range(8):
        bb, hf = c // 2, c % 2
        qbase = hf * T_OWN
        m = np.arange(EBW)
        Drel = np.clip(4095 - qbase - m, -4095, 4095)
        bk = _bucket(Drel)
        ohT = np.zeros((32, EBW), np.float32)
        ohT[bk, m] = 1.0
        hmask = np.array([1.0 if hf == 1 else 0.0, 1.0 if hf == 0 else 0.0], np.float32)
        per_core.append({
            "x_own": np.ascontiguousarray(x[bb, qbase:qbase + T_OWN]),
            "Wqkv0": np.asarray(inputs["L0_attn_W_qkv"], np.float32),
            "qg": np.asarray(inputs["L0_attn_q_gamma"], np.float32),
            "qb": np.asarray(inputs["L0_attn_q_beta"], np.float32),
            "kg": np.asarray(inputs["L0_attn_k_gamma"], np.float32),
            "kb": np.asarray(inputs["L0_attn_k_beta"], np.float32),
            "Wproj0": np.asarray(inputs["L0_attn_W_proj"], np.float32),
            "bproj0": np.asarray(inputs["L0_attn_b_proj"], np.float32),
            "Wffn0": np.asarray(inputs["L0_ffn_weight"], np.float32),
            "bffn0": np.asarray(inputs["L0_ffn_bias"], np.float32),
            "Wqkv1": np.asarray(inputs["L1_attn_w_qkv"], np.float32),
            "bqkv1": np.asarray(inputs["L1_attn_b_qkv"], np.float32),
            "Wdense1": np.asarray(inputs["L1_attn_w_dense"], np.float32),
            "bdense1": np.asarray(inputs["L1_attn_b_dense"], np.float32),
            "tabT": tabT,
            "w1": np.asarray(inputs["L1_ffn_w1"], np.float32),
            "convw": convw,
            "convb": np.asarray(inputs["L1_ffn_conv_b"], np.float32),
            "w2": np.asarray(inputs["L1_ffn_w2"], np.float32),
            "ohT": ohT,
            "ident": ident_b,
            "identf": ident_f,
            "colind2_in": colind2,
            "hmask": hmask,
        })
    return per_core


def kernel(**inputs):
    per_core = _host_inputs(inputs)
    nc, _ = build_nc()
    res = run_bass_kernel_spmd(nc, per_core, core_ids=list(range(8)))
    out = np.empty((B, S, D), dtype=np.float32)
    for c in range(8):
        bb, hf = c // 2, c % 2
        out[bb, hf * T_OWN:(hf + 1) * T_OWN] = res.results[c]["y"]
    return out
